# version 13
# baseline (speedup 1.0000x reference)
import numpy as np
import ml_dtypes
import concourse.bass as bass
import concourse.mybir as mybir
from concourse.bass_utils import run_bass_kernel_spmd

F32 = mybir.dt.float32
BF16 = mybir.dt.bfloat16
AF = mybir.ActivationFunctionType
ALU = mybir.AluOpType
AX = mybir.AxisListType.X

WNAMES = ["wq", "wk", "wv", "wo", "wr", "wkf", "wvf"]

SEMS = ["pe", "dve", "act", "gps",
        "dM", "dW2", "dW5", "dWTa", "dWTb",
        "dX0", "dX1", "dXN0", "dXN1", "dTP0", "dTP1",
        "dX20", "dX21", "dACC0", "dACC1"]
ENG_SEM = {"pe": "pe", "dve": "dve", "act": "act", "gps": "gps"}


class Em:
    ENGINES = ("pe", "dve", "act", "gps", "sync")

    def __init__(self):
        self.streams = {e: [] for e in self.ENGINES}
        self.tot = {s: 0 for s in SEMS}

    def emit(self, eng, fn, waits=(), inc=None, fwaits=()):
        ws = []
        for w in waits:
            if w is None:
                continue
            nm, v = w
            if v <= 0:
                continue
            ws.append((nm, v))
        for w in fwaits:
            if w is None:
                continue
            nm, v = w
            if v > 0:
                ws.append((nm, v))
        ev = None
        if inc is not None:
            if isinstance(inc, str):
                inc = (inc, 1)
            self.tot[inc[0]] += inc[1]
            ev = (inc[0], self.tot[inc[0]])
        self.streams[eng].append((fn, ws, inc))
        return ev


def _dtsize(dt):
    return 2 if dt == BF16 else 4


def build(S, T, dbg=False):
    C, NCH, TT = 1024, 8, 512
    assert T % TT == 0
    NT = T // TT
    NG = T // 128
    GPT = TT // 128

    nc = bass.Bass("TRN2", target_bir_lowering=False)

    x_h = nc.dram_tensor("x", [S, T, C], F32, kind="ExternalInput").ap()
    w_h = {nm: nc.dram_tensor(nm, [C, C], F32, kind="ExternalInput").ap()
           for nm in WNAMES}
    bo_h = nc.dram_tensor("bo", [1, C], F32, kind="ExternalInput").ap()
    mur_h = nc.dram_tensor("mur", [128, NCH], F32, kind="ExternalInput").ap()
    muk_h = nc.dram_tensor("muk", [128, NCH], F32, kind="ExternalInput").ap()
    bcm_h = nc.dram_tensor("bcm", [16, NCH, 128], BF16, kind="ExternalInput").ap()
    y_h = nc.dram_tensor("y", [S, T, C], F32, kind="ExternalOutput").ap()
    ident_h = nc.dram_tensor("ident", [128, 128], BF16,
                             kind="ExternalInput").ap()
    xnT_h = nc.dram_tensor("xnt", [S, NCH, 128, T], BF16, kind="Internal").ap()
    dbg_h = {}
    if dbg:
        for nm in ["xsum", "sumsq", "mean", "tmp", "negvar", "lnv", "rstd",
                   "nmr"]:
            dbg_h[nm] = nc.dram_tensor("dbg_" + nm, [128, 1], F32,
                                       kind="ExternalOutput").ap()
        for nm, shp, dt in [("qm", [128, NCH, 512], BF16),
                            ("xr", [128, NCH, 512], BF16),
                            ("xk", [128, NCH, 512], BF16),
                            ("k_act", [128, NCH, 512], BF16),
                            ("recf", [16, 512], F32),
                            ("kvblk", [128, NCH, 128], BF16),
                            ("km", [128, NCH, 16], BF16),
                            ("WTwkf", [128, NCH, 1024], BF16),
                            ("WTwq", [128, NCH, 1024], BF16),
                            ("WTwk", [128, NCH, 1024], BF16)]:
            dbg_h[nm] = nc.dram_tensor("dbg_" + nm, shp, dt,
                                       kind="ExternalOutput").ap()
    wbf_h = {nm: nc.dram_tensor("wbf_" + nm, [C, C], BF16, kind="Internal").ap()
             for nm in WNAMES}

    SBUF_LO, SBUF_HI = 16576, 229344
    cur = [SBUF_LO]

    sb_off = {}

    def sb(name, shape, dtype, cursor=None):
        cu = cursor if cursor is not None else cur
        nbytes = int(np.prod(shape[1:])) * _dtsize(dtype)
        off = (cu[0] + 63) & ~63
        cu[0] = off + nbytes
        sb_off[name] = off
        h = nc.alloc_sbuf_tensor_at(name, list(shape), dtype, offset=off)
        return h.ap()

    WT = {nm: sb("WT_" + nm, [128, NCH, 1024], BF16)
          for nm in WNAMES if nm not in ("wk", "wv")}
    XF = [sb("XF%d" % i, [128, NCH, 528], BF16) for i in range(2)]
    aT = [sb("aT%d" % i, [128, 512], F32) for i in range(2)]
    eT = [sb("eT%d" % i, [128, 512], F32) for i in range(2)]
    kvblk = [sb("kvblk%d" % s, [128, NCH, 128], BF16) for s in range(S)]
    km = [sb("km%d" % s, [128, NCH, 16], BF16) for s in range(S)]
    mur_sb = sb("mur_sb", [128, NCH], F32)
    muk_sb = sb("muk_sb", [128, NCH], F32)
    ones_bf = sb("ones_bf", [128, 1], BF16)
    ones_f32 = sb("ones_f32", [1, 128], F32)
    bcm_sb = sb("bcm_sb", [16, NCH, 128], BF16)
    epsln = sb("epsln", [128, 1], F32)
    bo_sb = sb("bo_sb", [1, C], F32)
    bo_bc = sb("bo_bc", [128, C], F32)
    identsb = sb("identsb", [128, 128], BF16)

    phase_base = (cur[0] + 63) & ~63

    c1 = [phase_base]
    for nm in ("wk", "wv"):
        WT[nm] = sb("WT_" + nm, [128, NCH, 1024], BF16, cursor=c1)
    x_tm = [sb("x_tm%d" % i, [128, C], F32, cursor=c1) for i in range(2)]
    xn_tm = [sb("xn_tm%d" % i, [128, C], BF16, cursor=c1) for i in range(4)]
    kf_tm = [sb("kf_tm%d" % i, [128, C], BF16, cursor=c1) for i in range(2)]
    v_tm = [sb("v_tm%d" % i, [128, C], BF16, cursor=c1) for i in range(2)]
    stn = ["xsum", "sumsq", "mean", "tmp", "negvar", "lnv", "rstd", "nmr"]
    st = {nm: [sb("st_%s%d" % (nm, i), [128, 1], F32, cursor=c1)
               for i in range(2)] for nm in stn}
    assert c1[0] <= SBUF_HI, c1[0]

    c2 = [phase_base]
    qm = sb("qm", [128, NCH, 512], BF16, cursor=c2)
    xr = [sb("xr%d" % i, [128, NCH, 512], BF16, cursor=c2) for i in range(2)]
    xk = [sb("xk%d" % i, [128, NCH, 512], BF16, cursor=c2) for i in range(2)]
    k_act = sb("k_act", [128, NCH, 512], BF16, cursor=c2)
    diffT = sb("diffT", [128, 512], F32, cursor=c2)
    t2 = [sb("t2_%d" % i, [128, 512], F32, cursor=c2) for i in range(2)]
    rb = [sb("rb%d" % i, [128, 512], BF16, cursor=c2) for i in range(2)]
    tt3 = [sb("tt3_%d" % i, [128, C], F32, cursor=c2) for i in range(2)]
    recf = sb("recf", [16, 512], F32, cursor=c2)
    recb = nc.alloc_sbuf_tensor_at(
        "recb", [16, 512], BF16, offset=sb_off["eT0"]).ap()
    x2 = sb("x2", [128, C], F32, cursor=c2)
    poolr = [sb("poolr_%d" % i, [128, 512], F32, cursor=c2) for i in range(2)]
    poolr2 = poolr
    assert c2[0] <= SBUF_HI, c2[0]

    PS = [nc.alloc_psum_tensor("B%d" % i, [128, 512], F32).ap() for i in range(8)]
    P7bf = PS[7].bitcast(BF16)

    sems = {nm: nc.alloc_semaphore(name=nm) for nm in SEMS}

    em = Em()
    E = em.emit

    E("sync", lambda e: e.dma_start(bo_sb, bo_h), inc=("dM", 16))
    E("sync", lambda e: e.dma_start(mur_sb, mur_h), inc=("dM", 16))
    E("sync", lambda e: e.dma_start(bcm_sb, bcm_h), inc=("dM", 16))
    E("sync", lambda e: e.dma_start(identsb, ident_h), inc=("dM", 16))
    ev_mu = E("sync", lambda e: e.dma_start(muk_sb, muk_h), inc=("dM", 16))

    E("dve", lambda e: e.memset(epsln, 1e-5))
    E("dve", lambda e: e.memset(ones_f32, 1.0))
    E("dve", lambda e: e.memset(ones_bf, 1.0))
    for s in range(S):
        E("dve", lambda e, s=s: e.memset(kvblk[s], 0.0))
        ev_ms = E("dve", lambda e, s=s: e.memset(km[s], 0.0),
                  inc="dve" if s == S - 1 else None)

    evb = [None, None]
    for h in range(2):
        evb[h] = E("pe", lambda e, h=h: e.matmul(
            PS[6 + h][:, :], ones_f32, bo_sb[:, h * 512:(h + 1) * 512],
            start=True, stop=True),
            waits=[ev_mu, ev_ms], inc="pe")
    ev_c6 = E("dve", lambda e: e.tensor_copy(bo_bc[:, 0:512], PS[6][:, :]),
              waits=[evb[0]], inc="dve")
    ev_bobc = E("dve", lambda e: e.tensor_copy(bo_bc[:, 512:1024], PS[7][:, :]),
                waits=[evb[1]], inc="dve")

    ev_cast2 = None
    for nm in ("wk", "wv"):
        ev_cast2 = E("gps", lambda e, nm=nm: e.dma_start(wbf_h[nm], w_h[nm]),
                     inc=("dW2", 16))

    def emit_wtp(nm, sem, cast_ev):
        last = None
        for c in range(NCH):
            last = E("sync", lambda e, nm=nm, c=c: e.dma_start(
                WT[nm][:, c, 0:1024], wbf_h[nm][:, c * 128:(c + 1) * 128],
                transpose=True),
                waits=[cast_ev] if c == 0 else (), inc=(sem, 16))
        return last

    ev_wTa = None

    p2w = ["wq", "wo", "wr", "wkf", "wvf"]
    N5 = 16 * len(p2w)
    cast_sched = {1 + j: nm for j, nm in enumerate(p2w)}
    tp_sched = {}
    total_tiles = S * NT
    for j, nm in enumerate(p2w):
        tp_sched.setdefault(min(2 + j, total_tiles - 1), []).append(nm)
    pending_casts = list(p2w)
    pending_tps = list(p2w)

    tb = [0]
    xf_free = {0: [], 1: []}
    bfree = {i: None for i in range(8)}
    bfree[6] = ev_c6
    bfree[7] = ev_bobc
    st_cnt = {"dXN0": 0, "dXN1": 0, "dX20": 0, "dX21": 0,
              "dACC0": 0, "dACC1": 0}

    h_sq = {}; h_tmp = {}; h_nv = {}; h_ln = {}; h_ap = {}; h_ya = {}
    h_kf = {}; h_vv = {}; h_kv = {}
    ev_ext = [None] * S

    dbg_gate = [None]

    t_end = {}

    def stage_group(s, g, gg):
        p = gg % 2
        p4 = gg % 4
        wl = [h_sq.get(gg - 2), h_ap.get(gg - 2)]
        if gg == 2 and dbg_gate[0] is not None:
            wl.append(dbg_gate[0])
        ev_x = E("sync", lambda e, s=s, g=g, p=p: e.dma_start(
            x_tm[p], x_h[s, g * 128:(g + 1) * 128, :]),
            waits=wl, inc=("dX%d" % p, 16))
        h_sq[gg] = E("act", lambda e, p=p, p4=p4: e.activation(
            xn_tm[p4], x_tm[p], AF.Square, accum_out=st["sumsq"][p]),
            waits=[ev_x, t_end.get(gg - 4), h_tmp.get(gg - 2)], inc="act")
        h_id = E("act", lambda e, p=p, p4=p4: e.activation(
            xn_tm[p4], x_tm[p], AF.Identity, accum_out=st["xsum"][p]),
            inc="act", fwaits=[h_sq[gg]])
        ev_mean = E("dve", lambda e, p=p: e.tensor_scalar(
            st["mean"][p], st["xsum"][p], 1.0 / C, None, ALU.mult),
            waits=[h_id], inc="dve")
        h_tmp[gg] = E("dve", lambda e, p=p: e.tensor_scalar(
            st["tmp"][p], st["sumsq"][p], -1.0 / C, None, ALU.mult),
            waits=[h_sq[gg]], inc="dve")
        h_nv[gg] = E("dve", lambda e, p=p: e.scalar_tensor_tensor(
            st["negvar"][p], st["mean"][p], st["mean"][p], st["tmp"][p],
            ALU.mult, ALU.add),
            waits=[h_ln.get(gg - 2)], inc="dve",
            fwaits=[ev_mean, h_tmp[gg]])
        h_ln[gg] = E("act", lambda e, p=p: e.activation(
            st["lnv"][p], st["negvar"][p], AF.Ln, scale=-1.0, bias=epsln),
            waits=[h_nv[gg], h_ap.get(gg - 2), ev_ms], inc="act")
        ev_rstd = E("act", lambda e, p=p: e.activation(
            st["rstd"][p], st["lnv"][p], AF.Exp, scale=-0.5), inc="act",
            fwaits=[h_ln[gg]])
        ev_nmr = E("dve", lambda e, p=p: e.tensor_scalar(
            st["nmr"][p], st["mean"][p], st["rstd"][p], -1.0,
            ALU.mult, ALU.mult),
            waits=[ev_rstd], inc="dve")
        h_ap[gg] = E("dve", lambda e, p=p, p4=p4: e.tensor_scalar(
            xn_tm[p4], x_tm[p], st["rstd"][p], st["nmr"][p],
            ALU.mult, ALU.add),
            waits=[t_end.get(gg - 4)], inc="dve", fwaits=[ev_nmr])
        if dbg and gg == 0:
            ev_dump = None
            for nm in stn:
                ev_dump = E("sync", lambda e, nm=nm: e.dma_start(
                    dbg_h[nm], st[nm][0]), waits=[h_ap[gg]], inc=("dM", 16))
            dbg_gate[0] = ev_dump

    bfree7 = [ev_bobc]

    def t_group(q, g_in, gg):
        p4 = gg % 4
        col = 16 + 128 * g_in
        w0 = [h_ap[gg], bfree7[0], ev_mu]
        ev_t = None
        for c in range(NCH):
            ev_t = E("pe", lambda e, c=c, p4=p4: e.transpose(
                P7bf[:, c * 128:(c + 1) * 128],
                xn_tm[p4][:, c * 128:(c + 1) * 128], identsb),
                waits=w0 if c == 0 else (),
                inc="pe" if c == NCH - 1 else None)
        t_end[gg] = ev_t
        wA = [ev_t] + (list(xf_free[q]) if g_in == 0 else [])
        evA = E("dve", lambda e, q=q, col=col: e.tensor_copy(
            XF[q][:, 0:4, col:col + 128], P7bf[:, 0:512]),
            waits=wA, inc="dve")
        evB = E("dve", lambda e, q=q, col=col: e.tensor_copy(
            XF[q][:, 4:8, col:col + 128], P7bf[:, 512:1024]),
            inc="dve", fwaits=[evA])
        bfree7[0] = evB
        return evA, evB

    def tile_load_xnt(s, i):
        q = tb[0] % 2
        tb[0] += 1
        wfirst = list(xf_free[q])
        t0 = i * TT
        last = None
        for c in range(NCH):
            if i == 0:
                src = xnT_h[s, c, 0:128, 0:TT]
                dst = XF[q][:, c, 16:528]
            else:
                src = xnT_h[s, c, 0:128, t0 - 16:t0 + TT]
                dst = XF[q][:, c, 0:528]
            last = E("sync", lambda e, src=src, dst=dst: e.dma_start(
                dst, src),
                waits=wfirst if c == 0 else (), inc=("dTP%d" % q, 16))
        evs = [last]
        if i == 0:
            evs.append(E("dve", lambda e, q=q: e.memset(XF[q][:, :, 0:16], 0.0),
                         waits=list(xf_free[q]), inc="dve"))
        return q, evs

    def p1_proj(q, evAB, g_in, gg):
        p = gg % 2
        evs = {}
        for h in range(2):
            kb, vb = h, 2 + h
            w0 = [evAB[0], ev_wTa, bfree[kb]]
            ev = None
            for c in range(NCH):
                ww = w0 if c == 0 else ([evAB[1]] if (c == 4 and h == 0)
                                        else ())
                ev = E("pe", lambda e, q=q, c=c, g_in=g_in, h=h, kb=kb: e.matmul(
                    PS[kb][:, :],
                    XF[q][:, c, 16 + 128 * g_in:16 + 128 * g_in + 128],
                    WT["wk"][:, c, h * 512:(h + 1) * 512],
                    start=(c == 0), stop=(c == NCH - 1)),
                    waits=ww,
                    inc="pe" if c == NCH - 1 else None)
            evs[("k", h)] = ev
            w0 = [bfree[vb]]
            for c in range(NCH):
                ev = E("pe", lambda e, q=q, c=c, g_in=g_in, h=h, vb=vb: e.matmul(
                    PS[vb][:, :],
                    XF[q][:, c, 16 + 128 * g_in:16 + 128 * g_in + 128],
                    WT["wv"][:, c, h * 512:(h + 1) * 512],
                    start=(c == 0), stop=(c == NCH - 1)),
                    waits=w0 if c == 0 else (),
                    inc="pe" if c == NCH - 1 else None)
            evs[("v", h)] = ev
        for h in range(2):
            kb, vb = h, 2 + h
            a_ev = E("dve", lambda e, h=h, kb=kb: e.tensor_scalar(
                aT[h], PS[kb][:, :], 0.0, None, ALU.min),
                waits=[evs[("k", h)]], inc="dve")
            e_ev = E("act", lambda e, h=h: e.activation(
                eT[h], aT[h], AF.Exp),
                waits=[a_ev, h_kf.get((gg - 1, h))], inc="act")
            h_kf[(gg, h)] = E("dve", lambda e, p=p, h=h, kb=kb:
                              e.scalar_tensor_tensor(
                                  kf_tm[p][:, h * 512:(h + 1) * 512],
                                  PS[kb][:, :], 0.0, eT[h], ALU.max, ALU.add),
                              waits=[e_ev, h_kv.get(gg - 2)], inc="dve")
            bfree[kb] = h_kf[(gg, h)]
            vv = E("dve", lambda e, p=p, h=h, vb=vb: e.tensor_copy(
                v_tm[p][:, h * 512:(h + 1) * 512], PS[vb][:, :]),
                waits=[evs[("v", h)], h_kv.get(gg - 2)], inc="dve")
            bfree[vb] = vv
            h_vv[(gg, h)] = vv

    def p1_kv(s, g, gg):
        p = gg % 2
        first = (g == 0)
        last_g = (g == NG - 1)
        w0 = [h_kf[(gg, 0)], h_kf[(gg, 1)], h_vv[(gg, 0)], h_vv[(gg, 1)]]
        if first:
            w0 += [bfree[4], bfree[5], bfree[6], ev_ms]
        ev = None
        for pr in range(NCH):
            bank = 4 + pr // 4
            col = (pr % 4) * 128
            E("pe", lambda e, p=p, pr=pr, bank=bank, col=col,
              st_=(first and pr % 4 == 0),
              sp=(last_g and pr % 4 == 3): e.matmul(
                  PS[bank][:, col:col + 128],
                  kf_tm[p][:, pr * 128:(pr + 1) * 128],
                  v_tm[p][:, pr * 128:(pr + 1) * 128],
                  start=st_, stop=sp),
              waits=w0 if pr == 0 else ())
            ev = E("pe", lambda e, p=p, pr=pr, st_=(first and pr == 0),
                   sp=(last_g and pr == NCH - 1): e.matmul(
                       PS[6][:, pr:pr + 1],
                       kf_tm[p][:, pr * 128:(pr + 1) * 128],
                       ones_bf, start=st_, stop=sp),
                   inc="pe" if pr == NCH - 1 else None)
        h_kv[gg] = ev
        return ev

    def p1_extract(s, ev_kv_last):
        last = None
        w0 = [ev_kv_last]
        for pr in range(NCH):
            bank = 4 + pr // 4
            col = (pr % 4) * 128
            E("dve", lambda e, s=s, pr=pr, bank=bank, col=col: e.tensor_copy(
                kvblk[s][0:64, pr, 0:64], PS[bank][0:64, col:col + 64]),
                waits=w0 if pr == 0 else ())
            E("dve", lambda e, s=s, pr=pr, bank=bank, col=col: e.tensor_copy(
                kvblk[s][64:128, pr, 64:128],
                PS[bank][64:128, col + 64:col + 128]))
            E("dve", lambda e, s=s, pr=pr: e.tensor_copy(
                km[s][0:64, pr, 2 * pr:2 * pr + 1], PS[6][0:64, pr:pr + 1]))
            last = E("dve", lambda e, s=s, pr=pr: e.tensor_copy(
                km[s][64:128, pr, 2 * pr + 1:2 * pr + 2],
                PS[6][64:128, pr:pr + 1]),
                inc="dve" if pr == NCH - 1 else None)
        bfree[4] = bfree[5] = bfree[6] = last
        return last

    stage_idx = [0]
    ntiles = S * NT

    def do_stage(tj, g_in):
        sj, ij = divmod(tj, NT)
        gj = ij * GPT + g_in
        stage_group(sj, gj, tj * GPT + g_in)
        sidx = stage_idx[0]
        stage_idx[0] += 1
        if sidx in cast_sched and cast_sched[sidx] in pending_casts:
            nm = cast_sched[sidx]
            pending_casts.remove(nm)
            E("gps", lambda e, nm=nm: e.dma_start(wbf_h[nm], w_h[nm]),
              inc=("dW5", 16))

    for g_in in range(GPT):
        do_stage(0, g_in)
        for nm in ("wk", "wv"):
            for c in (2 * g_in, 2 * g_in + 1):
                ev_wTa = E("sync", lambda e, nm=nm, c=c: e.dma_start(
                    WT[nm][:, c, 0:1024], wbf_h[nm][:, c * 128:(c + 1) * 128],
                    transpose=True),
                    waits=[ev_cast2] if (g_in == 0 and nm == "wk" and c == 0)
                    else (), inc=("dWTa", 16))
    ngroups = ntiles * GPT
    evabs = {0: t_group(0, 0, 0)}
    prev = None
    for ti in range(ntiles):
        s, i = divmod(ti, NT)
        q = ti % 2
        evac4 = None
        for g_in in range(GPT):
            g = i * GPT + g_in
            gg = ti * GPT + g_in
            if gg + 1 < ngroups:
                tj = (gg + 1) // GPT
                evabs[gg + 1] = t_group(tj % 2, (gg + 1) % GPT, gg + 1)
            evAB = evabs.pop(gg)
            evac4 = evAB[1]
            p1_proj(q, evAB, g_in, gg)
            if ti + 1 < ntiles:
                do_stage(ti + 1, g_in)
            if prev is not None:
                p1_kv(prev[0], prev[1], prev[2])
                if prev[1] == NG - 1:
                    ev_ext[prev[0]] = p1_extract(prev[0], h_kv[prev[2]])
                prev = None
            prev = (s, g, gg)
        for c in range(NCH):
            E("gps", lambda e, s=s, i=i, c=c, q=q: e.dma_start(
                xnT_h[s, c, 0:128, i * TT:(i + 1) * TT],
                XF[q][:, c, 16:528]),
                waits=[evac4] if c == 0 else (), inc=("dXN%d" % q, 16))
            st_cnt["dXN%d" % q] += 1
        xf_free[q] = [("pe", em.tot["pe"]),
                      ("dXN%d" % q, 16 * st_cnt["dXN%d" % q])]
        if not pending_casts:
            for nm in tp_sched.get(ti, []):
                if nm in pending_tps:
                    pending_tps.remove(nm)
                    emit_wtp(nm, "dWTb", ("dW5", N5))
    p1_kv(prev[0], prev[1], prev[2])
    ev_ext[prev[0]] = p1_extract(prev[0], h_kv[prev[2]])

    for nm in list(pending_casts):
        pending_casts.remove(nm)
        E("gps", lambda e, nm=nm: e.dma_start(wbf_h[nm], w_h[nm]),
          inc=("dW5", 16))
    for nm in list(pending_tps):
        pending_tps.remove(nm)
        emit_wtp(nm, "dWTb", ("dW5", N5))
    ev_wTb = ("dWTb", em.tot["dWTb"])

    bar = [(nm, em.tot[nm]) for nm in SEMS if em.tot[nm] > 0]
    for eng in Em.ENGINES:
        E(eng, None, waits=bar)

    h_t2 = {}
    h_t3 = {}
    h_rg = {}
    ghalf = [0]
    tps = {}
    mix_ev = {}

    mix_prev = [None]

    def emit_mix(tj, c):
        qj, evtpj = tps[tj]
        ev_d = E("dve", lambda e, qj=qj, c=c: e.tensor_tensor(
            diffT, XF[qj][:, c, 16:528], XF[qj][:, c, 15:527],
            ALU.subtract),
            waits=(list(evtpj) + [ev_mu]) if c == 0 else (),
            inc="dve", fwaits=[mix_prev[0]])
        E("dve", lambda e, qj=qj, c=c: e.scalar_tensor_tensor(
            xr[qj][:, c, 0:512], diffT, mur_sb[:, c:c + 1],
            XF[qj][:, c, 15:527], ALU.mult, ALU.add),
            fwaits=[ev_d])
        ev = E("dve", lambda e, qj=qj, c=c: e.scalar_tensor_tensor(
            xk[qj][:, c, 0:512], diffT, muk_sb[:, c:c + 1],
            XF[qj][:, c, 15:527], ALU.mult, ALU.add),
            inc="dve", fwaits=[ev_d])
        mix_prev[0] = ev
        if c == NCH - 1:
            mix_ev[tj] = ev

    ntiles2 = S * NT
    tps[0] = tile_load_xnt(0, 0)
    for c in range(NCH):
        emit_mix(0, c)

    for s in range(S):
        for i in range(NT):
            ti = s * NT + i
            if ti + 1 < ntiles2:
                s2, i2 = divmod(ti + 1, NT)
                tps[ti + 1] = tile_load_xnt(s2, i2)
            q, ev_tp = tps[ti]
            v_mix = mix_ev[ti]
            v_qf = {}
            for dc in range(NCH):
                bank = dc % 4
                w0 = list(ev_tp) + [ev_wTb, bfree[bank]]
                ev_q = None
                for c in range(NCH):
                    ev_q = E("pe", lambda e, q=q, c=c, dc=dc, bank=bank:
                             e.matmul(
                                 PS[bank][:, :],
                                 WT["wq"][:, c, dc * 128:(dc + 1) * 128],
                                 XF[q][:, c, 16:528],
                                 start=(c == 0), stop=(c == NCH - 1)),
                             waits=w0 if c == 0 else (),
                             inc="pe" if c == NCH - 1 else None)
                hp = dc % 2
                a_ev = E("dve", lambda e, hp=hp, bank=bank: e.tensor_scalar(
                    aT[hp], PS[bank][:, :], 0.0, None, ALU.min),
                    waits=[ev_q], inc="dve")
                e_ev = E("act", lambda e, hp=hp: e.activation(
                    eT[hp], aT[hp], AF.Exp), waits=[a_ev], inc="act")
                v_qf[dc] = E("dve", lambda e, dc=dc, hp=hp, bank=bank:
                             e.scalar_tensor_tensor(
                                 qm[:, dc, 0:512], PS[bank][:, :], 0.0,
                                 eT[hp], ALU.max, ALU.add),
                             waits=[e_ev], inc="dve")
                bfree[bank] = v_qf[dc]
            ev_aend = ("pe", em.tot["pe"])
            ev_den = None
            for dc in range(NCH):
                ev_den = E("pe", lambda e, s=s, dc=dc: e.matmul(
                    PS[6][0:16, :], km[s][:, dc, 0:16], qm[:, dc, 0:512],
                    start=(dc == 0), stop=(dc == NCH - 1)),
                    waits=([v_qf[dc], bfree[6], ev_ext[s]] if dc == 0
                           else [v_qf[dc]]),
                    inc="pe" if dc == NCH - 1 else None)
            a_ka = {}
            for dc in range(NCH):
                bank = dc % 2
                w0 = [v_mix, bfree[bank], ev_wTb]
                ev_k2 = None
                for c in range(NCH):
                    ev_k2 = E("pe", lambda e, q=q, c=c, dc=dc, bank=bank:
                              e.matmul(
                                  PS[bank][:, :],
                                  WT["wkf"][:, c, dc * 128:(dc + 1) * 128],
                                  xk[q][:, c, 0:512],
                                  start=(c == 0), stop=(c == NCH - 1)),
                              waits=w0 if c == 0 else (),
                              inc="pe" if c == NCH - 1 else None)
                ev_rl = E("act", lambda e, dc=dc, bank=bank: e.activation(
                    poolr[dc % 2], PS[bank][:, :], AF.Relu),
                    waits=[ev_k2], inc="act",
                    fwaits=[a_ka.get(dc - 2),
                            h_t2.get(ghalf[0] - 1) if dc < 2 else None])
                a_ka[dc] = E("act", lambda e, dc=dc: e.activation(
                    k_act[:, dc, 0:512], poolr[dc % 2], AF.Square), inc="act",
                    fwaits=[ev_rl])
                bfree[bank] = a_ka[dc]
            ev_eps = E("dve", lambda e: e.tensor_scalar(
                recf, PS[6][0:16, :], 1e-6, None, ALU.add), waits=[ev_den],
                inc="dve")
            ev_rc = E("dve", lambda e: e.reciprocal(recf, recf), inc="dve",
                      fwaits=[ev_eps])
            v_rec = E("dve", lambda e: e.tensor_copy(recb, recf), inc="dve",
                      fwaits=[ev_rc])
            bfree[6] = v_rec
            v_mg = None
            for dc in range(NCH):
                b2 = 2 + dc % 2
                b4 = 4 + dc % 2
                ev_qkv = E("pe", lambda e, s=s, dc=dc, b2=b2: e.matmul(
                    PS[b2][:, :], kvblk[s][:, dc, 0:128], qm[:, dc, 0:512],
                    start=True, stop=True),
                    waits=[bfree[b2], ev_ext[s]], inc="pe")
                ev_pbc = E("pe", lambda e, dc=dc, b4=b4: e.matmul(
                    PS[b4][:, :], bcm_sb[:, dc, 0:128], recb,
                    start=True, stop=True),
                    waits=[v_rec, bfree[b4], ev_mu], inc="pe")
                v_rb = E("dve", lambda e, dc=dc, b4=b4: e.tensor_copy(
                    rb[dc % 2], PS[b4][:, :]), waits=[ev_pbc], inc="dve")
                bfree[b4] = v_rb
                v_mg = E("dve", lambda e, dc=dc, b2=b2: e.tensor_tensor(
                    qm[:, dc, 0:512], PS[b2][:, :], rb[dc % 2], ALU.mult),
                    waits=[ev_qkv], inc="dve", fwaits=[v_rb])
                bfree[b2] = v_mg
            for g_in in range(GPT):
                ga = ti * GPT + g_in
                pa = ga % 2
                g = i * GPT + g_in
                E("gps", lambda e, s=s, g=g: e.dma_start(
                    x2, x_h[s, g * 128:(g + 1) * 128, :]),
                    inc=("dX20", 16))
                st_cnt["dX20"] += 1
                for h in range(2):
                    hi = ghalf[0]
                    ghalf[0] += 1
                    par = hi % 2
                    b_at = 2 + par
                    b_r = 4 + par
                    b_vf = 6 + par
                    ev_r = None
                    w0 = [v_mix, bfree[b_r]]
                    for c in range(NCH):
                        ev_r = E("pe", lambda e, q=q, c=c, g_in=g_in, h=h,
                                 b_r=b_r: e.matmul(
                                     PS[b_r][:, :],
                                     xr[q][:, c, g_in * 128:(g_in + 1) * 128],
                                     WT["wr"][:, c, h * 512:(h + 1) * 512],
                                     start=(c == 0), stop=(c == NCH - 1)),
                                 waits=w0 if c == 0 else (),
                                 inc="pe" if c == NCH - 1 else None)
                    ev_vf = None
                    w0 = [a_ka[NCH - 1], bfree[b_vf]]
                    for c in range(NCH):
                        ev_vf = E("pe", lambda e, c=c, g_in=g_in, h=h,
                                  b_vf=b_vf: e.matmul(
                                      PS[b_vf][:, :],
                                      k_act[:, c, g_in * 128:(g_in + 1) * 128],
                                      WT["wvf"][:, c, h * 512:(h + 1) * 512],
                                      start=(c == 0), stop=(c == NCH - 1)),
                                  waits=w0 if c == 0 else (),
                                  inc="pe" if c == NCH - 1 else None)
                    ev_at = None
                    w0 = [v_mg, bfree[b_at]]
                    for c in range(NCH):
                        ev_at = E("pe", lambda e, c=c, g_in=g_in, h=h,
                                  b_at=b_at: e.matmul(
                                      PS[b_at][:, :],
                                      qm[:, c, g_in * 128:(g_in + 1) * 128],
                                      WT["wo"][:, c, h * 512:(h + 1) * 512],
                                      start=(c == 0), stop=(c == NCH - 1)),
                                  waits=w0 if c == 0 else (),
                                  inc="pe" if c == NCH - 1 else None)
                    ev_th = E("act", lambda e, par=par, b_r=b_r: e.activation(
                        poolr2[par], PS[b_r][:, :], AF.Tanh, scale=0.5),
                        waits=[ev_r, h_t2.get(hi - 2), h_rg.get(hi - 2)],
                        inc="act",
                        fwaits=[a_ka[NCH - 1] if hi % 8 == 0 else None])
                    bfree[b_r] = ev_th
                    ev_rg = E("dve", lambda e, par=par: e.tensor_scalar(
                        poolr2[par], poolr2[par], 0.5, 0.5, ALU.mult, ALU.add),
                        waits=[ev_th], inc="dve")
                    h_rg[hi] = ev_rg
                    h_t2[hi] = E("dve", lambda e, par=par, b_vf=b_vf:
                                 e.tensor_tensor(
                                     t2[par], PS[b_vf][:, :], poolr2[par],
                                     ALU.mult),
                                 waits=[ev_vf], inc="dve", fwaits=[ev_rg])
                    bfree[b_vf] = h_t2[hi]
                    h_t3[(ga, h)] = E("dve", lambda e, pa=pa, h=h, par=par,
                                      b_at=b_at: e.tensor_tensor(
                                          tt3[pa][:, h * 512:(h + 1) * 512],
                                          t2[par], PS[b_at][:, :], ALU.add),
                                      waits=[ev_at,
                                             ("dACC%d" % pa,
                                              16 * st_cnt["dACC%d" % pa])],
                                      inc="dve", fwaits=[h_t2[hi]])
                    bfree[b_at] = h_t3[(ga, h)]
                    if ti + 1 < ntiles2:
                        emit_mix(ti + 1, hi % 8)
                ev_xb = E("gps", lambda e: e.tensor_tensor(
                    x2, x2, bo_bc, ALU.add),
                    waits=[("dX20", 16 * st_cnt["dX20"]), ev_bobc],
                    inc="gps")
                ev_t4 = E("gps", lambda e, pa=pa: e.tensor_tensor(
                    tt3[pa], tt3[pa], x2, ALU.add),
                    waits=[h_t3[(ga, 0)], h_t3[(ga, 1)]], inc="gps",
                    fwaits=[ev_xb])
                E("gps", lambda e, s=s, g=g, pa=pa: e.dma_start(
                    y_h[s, g * 128:(g + 1) * 128, :], tt3[pa]),
                    inc=("dACC%d" % pa, 16), fwaits=[ev_t4])
                st_cnt["dACC%d" % pa] += 1
            xf_free[q] = [ev_aend, v_mix]

    if dbg:
        allsem = [(nm, em.tot[nm]) for nm in ("pe", "dve", "act", "gps")]
        for nm, src in [("qm", qm), ("xr", xr[0]), ("xk", xk[0]),
                        ("k_act", k_act),
                        ("recf", recf), ("kvblk", kvblk[0]), ("km", km[0]),
                        ("WTwkf", WT["wkf"]), ("WTwq", WT["wq"]),
                        ("WTwk", WT["wk"])]:
            E("sync", lambda e, nm=nm, src=src: e.dma_start(dbg_h[nm], src),
              waits=allsem, inc=("dM", 16))

    fin = [("dACC0", em.tot["dACC0"]), ("dACC1", em.tot["dACC1"]),
           ("dM", em.tot["dM"])]
    for eng in Em.ENGINES:
        E(eng, None, waits=fin)

    with nc.Block() as block:
        def mk(eng_name):
            def f(eng):
                waited = {}
                for fn, ws, inc in em.streams[eng_name]:
                    for nm, v in ws:
                        if waited.get(nm, 0) < v:
                            eng.wait_ge(sems[nm], v)
                            waited[nm] = v
                    if fn is None:
                        continue
                    inst = fn(eng)
                    if inc is not None:
                        inst.then_inc(sems[inc[0]], inc[1])
            return f
        block.tensor(mk("pe"))
        block.vector(mk("dve"))
        block.scalar(mk("act"))
        block.gpsimd(mk("gps"))
        block.sync(mk("sync"))
    return nc


_NC_CACHE = {}


def _get_nc(S, T):
    key = (S, T)
    if key not in _NC_CACHE:
        _NC_CACHE[key] = build(S, T)
    return _NC_CACHE[key]


def kernel(L0_attn_x, L0_attn_wq, L0_attn_wk, L0_attn_wv, L0_attn_wo,
           L0_attn_bo, L0_ffn_W_r, L0_ffn_W_k, L0_ffn_W_v, L0_ffn_mu_r,
           L0_ffn_mu_k):
    x = np.ascontiguousarray(np.asarray(L0_attn_x, dtype=np.float32))
    B, T, C = x.shape
    n = 8
    S = B // n
    nc = _get_nc(S, T)

    def w(a):
        return np.ascontiguousarray(np.asarray(a, dtype=np.float32))

    ws = {"wq": w(L0_attn_wq), "wk": w(L0_attn_wk), "wv": w(L0_attn_wv),
          "wo": w(L0_attn_wo), "wr": w(L0_ffn_W_r), "wkf": w(L0_ffn_W_k),
          "wvf": w(L0_ffn_W_v)}
    bo = np.ascontiguousarray(w(L0_attn_bo).reshape(1, C))
    mur = np.ascontiguousarray(w(L0_ffn_mu_r).reshape(C // 128, 128).T)
    muk = np.ascontiguousarray(w(L0_ffn_mu_k).reshape(C // 128, 128).T)
    bcm = np.zeros((16, 8, 128), dtype=ml_dtypes.bfloat16)
    for dc in range(8):
        bcm[2 * dc, dc, 0:64] = 1
        bcm[2 * dc + 1, dc, 64:128] = 1
    ident = np.eye(128, dtype=ml_dtypes.bfloat16)

    in_maps = []
    for k in range(n):
        m = {"x": np.ascontiguousarray(x[k * S:(k + 1) * S])}
        m.update(ws)
        m["bo"] = bo
        m["mur"] = mur
        m["muk"] = muk
        m["bcm"] = bcm
        m["ident"] = ident
        in_maps.append(m)

    res = run_bass_kernel_spmd(nc, in_maps, core_ids=list(range(n)))
    out = np.concatenate([np.asarray(r["y"]) for r in res.results], axis=0)
    return out.astype(np.float32)


# version 14 (final)
# speedup vs baseline: 1.0246x; 1.0246x over previous
import numpy as np
import ml_dtypes
import concourse.bass as bass
import concourse.mybir as mybir
from concourse.bass_utils import run_bass_kernel_spmd

F32 = mybir.dt.float32
BF16 = mybir.dt.bfloat16
AF = mybir.ActivationFunctionType
ALU = mybir.AluOpType
AX = mybir.AxisListType.X

WNAMES = ["wq", "wk", "wv", "wo", "wr", "wkf", "wvf"]

SEMS = ["pe", "dve", "act", "gps",
        "dM", "dW2", "dW5", "dWTa", "dWTb",
        "dX0", "dX1", "dXN0", "dXN1", "dTP0", "dTP1",
        "dX20", "dX21", "dACC0", "dACC1"]
ENG_SEM = {"pe": "pe", "dve": "dve", "act": "act", "gps": "gps"}


class Em:
    ENGINES = ("pe", "dve", "act", "gps", "sync")

    def __init__(self):
        self.streams = {e: [] for e in self.ENGINES}
        self.tot = {s: 0 for s in SEMS}

    def emit(self, eng, fn, waits=(), inc=None, fwaits=()):
        ws = []
        for w in waits:
            if w is None:
                continue
            nm, v = w
            if v <= 0:
                continue
            ws.append((nm, v))
        for w in fwaits:
            if w is None:
                continue
            nm, v = w
            if v > 0:
                ws.append((nm, v))
        ev = None
        if inc is not None:
            if isinstance(inc, str):
                inc = (inc, 1)
            self.tot[inc[0]] += inc[1]
            ev = (inc[0], self.tot[inc[0]])
        self.streams[eng].append((fn, ws, inc))
        return ev


def _dtsize(dt):
    return 2 if dt == BF16 else 4


def build(S, T, dbg=False):
    C, NCH, TT = 1024, 8, 512
    assert T % TT == 0
    NT = T // TT
    NG = T // 128
    GPT = TT // 128

    nc = bass.Bass("TRN2", target_bir_lowering=False)

    x_h = nc.dram_tensor("x", [S, T, C], F32, kind="ExternalInput").ap()
    w_h = {nm: nc.dram_tensor(nm, [C, C], F32, kind="ExternalInput").ap()
           for nm in WNAMES}
    bo_h = nc.dram_tensor("bo", [1, C], F32, kind="ExternalInput").ap()
    mur_h = nc.dram_tensor("mur", [128, NCH], F32, kind="ExternalInput").ap()
    muk_h = nc.dram_tensor("muk", [128, NCH], F32, kind="ExternalInput").ap()
    bcm_h = nc.dram_tensor("bcm", [16, NCH, 128], BF16, kind="ExternalInput").ap()
    y_h = nc.dram_tensor("y", [S, T, C], F32, kind="ExternalOutput").ap()
    ident_h = nc.dram_tensor("ident", [128, 128], BF16,
                             kind="ExternalInput").ap()
    xnT_h = nc.dram_tensor("xnt", [S, NCH, 128, T], BF16, kind="Internal").ap()
    dbg_h = {}
    if dbg:
        for nm in ["xsum", "sumsq", "mean", "tmp", "negvar", "lnv", "rstd",
                   "nmr"]:
            dbg_h[nm] = nc.dram_tensor("dbg_" + nm, [128, 1], F32,
                                       kind="ExternalOutput").ap()
        for nm, shp, dt in [("qm", [128, NCH, 512], BF16),
                            ("xr", [128, NCH, 512], BF16),
                            ("xk", [128, NCH, 512], BF16),
                            ("k_act", [128, NCH, 512], BF16),
                            ("recf", [16, 512], F32),
                            ("kvblk", [128, NCH, 128], BF16),
                            ("km", [128, NCH, 16], BF16),
                            ("WTwkf", [128, NCH, 1024], BF16),
                            ("WTwq", [128, NCH, 1024], BF16),
                            ("WTwk", [128, NCH, 1024], BF16)]:
            dbg_h[nm] = nc.dram_tensor("dbg_" + nm, shp, dt,
                                       kind="ExternalOutput").ap()
    wbf_h = {nm: nc.dram_tensor("wbf_" + nm, [C, C], BF16, kind="Internal").ap()
             for nm in WNAMES}

    SBUF_LO, SBUF_HI = 16576, 229344
    cur = [SBUF_LO]

    sb_off = {}

    def sb(name, shape, dtype, cursor=None):
        cu = cursor if cursor is not None else cur
        nbytes = int(np.prod(shape[1:])) * _dtsize(dtype)
        off = (cu[0] + 63) & ~63
        cu[0] = off + nbytes
        sb_off[name] = off
        h = nc.alloc_sbuf_tensor_at(name, list(shape), dtype, offset=off)
        return h.ap()

    WT = {nm: sb("WT_" + nm, [128, NCH, 1024], BF16)
          for nm in WNAMES if nm not in ("wk", "wv")}
    XF = [sb("XF%d" % i, [128, NCH, 528], BF16) for i in range(2)]
    aT = [sb("aT%d" % i, [128, 512], F32) for i in range(2)]
    eT = [sb("eT%d" % i, [128, 512], F32) for i in range(2)]
    kvblk = [sb("kvblk%d" % s, [128, NCH, 128], BF16) for s in range(S)]
    km = [sb("km%d" % s, [128, NCH, 16], BF16) for s in range(S)]
    mur_sb = sb("mur_sb", [128, NCH], F32)
    muk_sb = sb("muk_sb", [128, NCH], F32)
    ones_bf = sb("ones_bf", [128, 1], BF16)
    ones_f32 = sb("ones_f32", [1, 128], F32)
    bcm_sb = sb("bcm_sb", [16, NCH, 128], BF16)
    epsln = sb("epsln", [128, 1], F32)
    bo_sb = sb("bo_sb", [1, C], F32)
    bo_bc = sb("bo_bc", [128, C], F32)
    identsb = sb("identsb", [128, 128], BF16)

    phase_base = (cur[0] + 63) & ~63

    c1 = [phase_base]
    for nm in ("wk", "wv"):
        WT[nm] = sb("WT_" + nm, [128, NCH, 1024], BF16, cursor=c1)
    x_tm = [sb("x_tm%d" % i, [128, C], F32, cursor=c1) for i in range(2)]
    xn_tm = [sb("xn_tm%d" % i, [128, C], BF16, cursor=c1) for i in range(4)]
    kf_tm = [sb("kf_tm%d" % i, [128, C], BF16, cursor=c1) for i in range(2)]
    v_tm = [sb("v_tm%d" % i, [128, C], BF16, cursor=c1) for i in range(2)]
    stn = ["xsum", "sumsq", "mean", "tmp", "negvar", "lnv", "rstd", "nmr"]
    st = {nm: [sb("st_%s%d" % (nm, i), [128, 1], F32, cursor=c1)
               for i in range(2)] for nm in stn}
    assert c1[0] <= SBUF_HI, c1[0]

    c2 = [phase_base]
    qm = sb("qm", [128, NCH, 512], BF16, cursor=c2)
    xr = [sb("xr%d" % i, [128, NCH, 512], BF16, cursor=c2) for i in range(2)]
    xk = [sb("xk%d" % i, [128, NCH, 512], BF16, cursor=c2) for i in range(2)]
    k_act = sb("k_act", [128, NCH, 512], BF16, cursor=c2)
    diffT = sb("diffT", [128, 512], F32, cursor=c2)
    t2 = [sb("t2_%d" % i, [128, 512], F32, cursor=c2) for i in range(2)]
    rb = [sb("rb%d" % i, [128, 512], BF16, cursor=c2) for i in range(2)]
    tt3 = [sb("tt3_%d" % i, [128, C], F32, cursor=c2) for i in range(2)]
    recf = sb("recf", [16, 512], F32, cursor=c2)
    recb = nc.alloc_sbuf_tensor_at(
        "recb", [16, 512], BF16, offset=sb_off["eT0"]).ap()
    x2 = sb("x2", [128, C], F32, cursor=c2)
    poolr = [sb("poolr_%d" % i, [128, 512], F32, cursor=c2) for i in range(2)]
    poolr2 = poolr
    assert c2[0] <= SBUF_HI, c2[0]

    PS = [nc.alloc_psum_tensor("B%d" % i, [128, 512], F32).ap() for i in range(8)]
    P7bf = PS[7].bitcast(BF16)

    sems = {nm: nc.alloc_semaphore(name=nm) for nm in SEMS}

    em = Em()
    E = em.emit

    E("sync", lambda e: e.dma_start(bo_sb, bo_h), inc=("dM", 16))
    E("sync", lambda e: e.dma_start(mur_sb, mur_h), inc=("dM", 16))
    E("sync", lambda e: e.dma_start(bcm_sb, bcm_h), inc=("dM", 16))
    E("sync", lambda e: e.dma_start(identsb, ident_h), inc=("dM", 16))
    ev_mu = E("sync", lambda e: e.dma_start(muk_sb, muk_h), inc=("dM", 16))

    E("dve", lambda e: e.memset(epsln, 1e-5))
    E("dve", lambda e: e.memset(ones_f32, 1.0))
    E("dve", lambda e: e.memset(ones_bf, 1.0))
    for s in range(S):
        E("dve", lambda e, s=s: e.memset(kvblk[s], 0.0))
        ev_ms = E("dve", lambda e, s=s: e.memset(km[s], 0.0),
                  inc="dve" if s == S - 1 else None)

    evb = [None, None]
    for h in range(2):
        evb[h] = E("pe", lambda e, h=h: e.matmul(
            PS[6 + h][:, :], ones_f32, bo_sb[:, h * 512:(h + 1) * 512],
            start=True, stop=True),
            waits=[ev_mu, ev_ms], inc="pe")
    ev_c6 = E("dve", lambda e: e.tensor_copy(bo_bc[:, 0:512], PS[6][:, :]),
              waits=[evb[0]], inc="dve")
    ev_bobc = E("dve", lambda e: e.tensor_copy(bo_bc[:, 512:1024], PS[7][:, :]),
                waits=[evb[1]], inc="dve")

    ev_cast2 = None
    for nm in ("wk", "wv"):
        ev_cast2 = E("gps", lambda e, nm=nm: e.dma_start(wbf_h[nm], w_h[nm]),
                     inc=("dW2", 16))

    def emit_wtp(nm, sem, cast_ev):
        last = None
        for c in range(NCH):
            last = E("sync", lambda e, nm=nm, c=c: e.dma_start(
                WT[nm][:, c, 0:1024], wbf_h[nm][:, c * 128:(c + 1) * 128],
                transpose=True),
                waits=[cast_ev] if c == 0 else (), inc=(sem, 16))
        return last

    ev_wTa = None

    p2w = ["wq", "wo", "wr", "wkf", "wvf"]
    N5 = 16 * len(p2w)
    cast_sched = {1 + j: nm for j, nm in enumerate(p2w)}
    tp_sched = {}
    total_tiles = S * NT
    for j, nm in enumerate(p2w):
        tp_sched.setdefault(min(2 + j, total_tiles - 1), []).append(nm)
    pending_casts = list(p2w)
    pending_tps = list(p2w)

    tb = [0]
    xf_free = {0: [], 1: []}
    bfree = {i: None for i in range(8)}
    bfree[6] = ev_c6
    bfree[7] = ev_bobc
    st_cnt = {"dXN0": 0, "dXN1": 0, "dX20": 0, "dX21": 0,
              "dACC0": 0, "dACC1": 0}

    h_sq = {}; h_tmp = {}; h_nv = {}; h_ln = {}; h_ap = {}; h_ya = {}
    h_kf = {}; h_vv = {}; h_kv = {}
    ev_ext = [None] * S

    dbg_gate = [None]

    t_end = {}

    def stage_group(s, g, gg):
        p = gg % 2
        p4 = gg % 4
        wl = [h_sq.get(gg - 2), h_ap.get(gg - 2)]
        if gg == 2 and dbg_gate[0] is not None:
            wl.append(dbg_gate[0])
        ev_x = E("sync", lambda e, s=s, g=g, p=p: e.dma_start(
            x_tm[p], x_h[s, g * 128:(g + 1) * 128, :]),
            waits=wl, inc=("dX%d" % p, 16))
        h_sq[gg] = E("act", lambda e, p=p, p4=p4: e.activation(
            xn_tm[p4], x_tm[p], AF.Square, accum_out=st["sumsq"][p]),
            waits=[ev_x, t_end.get(gg - 4), h_tmp.get(gg - 2)], inc="act")
        h_id = E("act", lambda e, p=p, p4=p4: e.activation(
            xn_tm[p4], x_tm[p], AF.Identity, accum_out=st["xsum"][p]),
            inc="act", fwaits=[h_sq[gg]])
        ev_mean = E("dve", lambda e, p=p: e.tensor_scalar(
            st["mean"][p], st["xsum"][p], 1.0 / C, None, ALU.mult),
            waits=[h_id], inc="dve")
        h_tmp[gg] = E("dve", lambda e, p=p: e.tensor_scalar(
            st["tmp"][p], st["sumsq"][p], -1.0 / C, None, ALU.mult),
            waits=[h_sq[gg]], inc="dve")
        h_nv[gg] = E("dve", lambda e, p=p: e.scalar_tensor_tensor(
            st["negvar"][p], st["mean"][p], st["mean"][p], st["tmp"][p],
            ALU.mult, ALU.add),
            waits=[h_ln.get(gg - 2)], inc="dve",
            fwaits=[ev_mean, h_tmp[gg]])
        h_ln[gg] = E("act", lambda e, p=p: e.activation(
            st["lnv"][p], st["negvar"][p], AF.Ln, scale=-1.0, bias=epsln),
            waits=[h_nv[gg], h_ap.get(gg - 2), ev_ms], inc="act")
        ev_rstd = E("act", lambda e, p=p: e.activation(
            st["rstd"][p], st["lnv"][p], AF.Exp, scale=-0.5), inc="act",
            fwaits=[h_ln[gg]])
        ev_nmr = E("dve", lambda e, p=p: e.tensor_scalar(
            st["nmr"][p], st["mean"][p], st["rstd"][p], -1.0,
            ALU.mult, ALU.mult),
            waits=[ev_rstd], inc="dve")
        h_ap[gg] = E("dve", lambda e, p=p, p4=p4: e.tensor_scalar(
            xn_tm[p4], x_tm[p], st["rstd"][p], st["nmr"][p],
            ALU.mult, ALU.add),
            waits=[t_end.get(gg - 4)], inc="dve", fwaits=[ev_nmr])
        if dbg and gg == 0:
            ev_dump = None
            for nm in stn:
                ev_dump = E("sync", lambda e, nm=nm: e.dma_start(
                    dbg_h[nm], st[nm][0]), waits=[h_ap[gg]], inc=("dM", 16))
            dbg_gate[0] = ev_dump

    bfree7 = [ev_bobc]

    def t_group(q, g_in, gg):
        p4 = gg % 4
        col = 16 + 128 * g_in
        w0 = [h_ap[gg], bfree7[0], ev_mu]
        ev_t = None
        for c in range(NCH):
            ev_t = E("pe", lambda e, c=c, p4=p4: e.transpose(
                P7bf[:, c * 128:(c + 1) * 128],
                xn_tm[p4][:, c * 128:(c + 1) * 128], identsb),
                waits=w0 if c == 0 else (),
                inc="pe" if c == NCH - 1 else None)
        t_end[gg] = ev_t
        wA = [ev_t] + (list(xf_free[q]) if g_in == 0 else [])
        evA = E("dve", lambda e, q=q, col=col: e.tensor_copy(
            XF[q][:, 0:4, col:col + 128], P7bf[:, 0:512]),
            waits=wA, inc="dve")
        evB = E("dve", lambda e, q=q, col=col: e.tensor_copy(
            XF[q][:, 4:8, col:col + 128], P7bf[:, 512:1024]),
            inc="dve", fwaits=[evA])
        bfree7[0] = evB
        return evA, evB

    def tile_load_xnt(s, i):
        q = tb[0] % 2
        tb[0] += 1
        wfirst = list(xf_free[q])
        t0 = i * TT
        last = None
        for c in range(NCH):
            if i == 0:
                src = xnT_h[s, c, 0:128, 0:TT]
                dst = XF[q][:, c, 16:528]
            else:
                src = xnT_h[s, c, 0:128, t0 - 16:t0 + TT]
                dst = XF[q][:, c, 0:528]
            last = E("sync", lambda e, src=src, dst=dst: e.dma_start(
                dst, src),
                waits=wfirst if c == 0 else (), inc=("dTP%d" % q, 16))
        evs = [last]
        if i == 0:
            evs.append(E("dve", lambda e, q=q: e.memset(XF[q][:, :, 0:16], 0.0),
                         waits=list(xf_free[q]), inc="dve"))
        return q, evs

    def p1_proj(q, evAB, g_in, gg):
        p = gg % 2
        evs = {}
        for h in range(2):
            kb, vb = h, 2 + h
            w0 = [evAB[0], ev_wTa, bfree[kb]]
            ev = None
            for c in range(NCH):
                ww = w0 if c == 0 else ([evAB[1]] if (c == 4 and h == 0)
                                        else ())
                ev = E("pe", lambda e, q=q, c=c, g_in=g_in, h=h, kb=kb: e.matmul(
                    PS[kb][:, :],
                    XF[q][:, c, 16 + 128 * g_in:16 + 128 * g_in + 128],
                    WT["wk"][:, c, h * 512:(h + 1) * 512],
                    start=(c == 0), stop=(c == NCH - 1)),
                    waits=ww,
                    inc="pe" if c == NCH - 1 else None)
            evs[("k", h)] = ev
            w0 = [bfree[vb]]
            for c in range(NCH):
                ev = E("pe", lambda e, q=q, c=c, g_in=g_in, h=h, vb=vb: e.matmul(
                    PS[vb][:, :],
                    XF[q][:, c, 16 + 128 * g_in:16 + 128 * g_in + 128],
                    WT["wv"][:, c, h * 512:(h + 1) * 512],
                    start=(c == 0), stop=(c == NCH - 1)),
                    waits=w0 if c == 0 else (),
                    inc="pe" if c == NCH - 1 else None)
            evs[("v", h)] = ev
        for h in range(2):
            kb, vb = h, 2 + h
            a_ev = E("dve", lambda e, h=h, kb=kb: e.tensor_scalar(
                aT[h], PS[kb][:, :], 0.0, None, ALU.min),
                waits=[evs[("k", h)]], inc="dve")
            e_ev = E("act", lambda e, h=h: e.activation(
                eT[h], aT[h], AF.Exp),
                waits=[a_ev, h_kf.get((gg - 1, h))], inc="act")
            h_kf[(gg, h)] = E("dve", lambda e, p=p, h=h, kb=kb:
                              e.scalar_tensor_tensor(
                                  kf_tm[p][:, h * 512:(h + 1) * 512],
                                  PS[kb][:, :], 0.0, eT[h], ALU.max, ALU.add),
                              waits=[e_ev, h_kv.get(gg - 2)], inc="dve")
            bfree[kb] = h_kf[(gg, h)]
            vv = E("dve", lambda e, p=p, h=h, vb=vb: e.tensor_copy(
                v_tm[p][:, h * 512:(h + 1) * 512], PS[vb][:, :]),
                waits=[evs[("v", h)], h_kv.get(gg - 2)], inc="dve")
            bfree[vb] = vv
            h_vv[(gg, h)] = vv

    def p1_kv(s, g, gg):
        p = gg % 2
        first = (g == 0)
        last_g = (g == NG - 1)
        w0 = [h_kf[(gg, 0)], h_kf[(gg, 1)], h_vv[(gg, 0)], h_vv[(gg, 1)]]
        if first:
            w0 += [bfree[4], bfree[5], bfree[6], ev_ms]
        ev = None
        for pr in range(NCH):
            bank = 4 + pr // 4
            col = (pr % 4) * 128
            E("pe", lambda e, p=p, pr=pr, bank=bank, col=col,
              st_=(first and pr % 4 == 0),
              sp=(last_g and pr % 4 == 3): e.matmul(
                  PS[bank][:, col:col + 128],
                  kf_tm[p][:, pr * 128:(pr + 1) * 128],
                  v_tm[p][:, pr * 128:(pr + 1) * 128],
                  start=st_, stop=sp),
              waits=w0 if pr == 0 else ())
            ev = E("pe", lambda e, p=p, pr=pr, st_=(first and pr == 0),
                   sp=(last_g and pr == NCH - 1): e.matmul(
                       PS[6][:, pr:pr + 1],
                       kf_tm[p][:, pr * 128:(pr + 1) * 128],
                       ones_bf, start=st_, stop=sp),
                   inc="pe" if pr == NCH - 1 else None)
        h_kv[gg] = ev
        return ev

    def p1_extract(s, ev_kv_last):
        last = None
        w0 = [ev_kv_last]
        for pr in range(NCH):
            bank = 4 + pr // 4
            col = (pr % 4) * 128
            E("dve", lambda e, s=s, pr=pr, bank=bank, col=col: e.tensor_copy(
                kvblk[s][0:64, pr, 0:64], PS[bank][0:64, col:col + 64]),
                waits=w0 if pr == 0 else ())
            E("dve", lambda e, s=s, pr=pr, bank=bank, col=col: e.tensor_copy(
                kvblk[s][64:128, pr, 64:128],
                PS[bank][64:128, col + 64:col + 128]))
            E("dve", lambda e, s=s, pr=pr: e.tensor_copy(
                km[s][0:64, pr, 2 * pr:2 * pr + 1], PS[6][0:64, pr:pr + 1]))
            last = E("dve", lambda e, s=s, pr=pr: e.tensor_copy(
                km[s][64:128, pr, 2 * pr + 1:2 * pr + 2],
                PS[6][64:128, pr:pr + 1]),
                inc="dve" if pr == NCH - 1 else None)
        bfree[4] = bfree[5] = bfree[6] = last
        return last

    stage_idx = [0]
    ntiles = S * NT

    def do_stage(tj, g_in):
        sj, ij = divmod(tj, NT)
        gj = ij * GPT + g_in
        stage_group(sj, gj, tj * GPT + g_in)
        sidx = stage_idx[0]
        stage_idx[0] += 1
        if sidx in cast_sched and cast_sched[sidx] in pending_casts:
            nm = cast_sched[sidx]
            pending_casts.remove(nm)
            E("gps", lambda e, nm=nm: e.dma_start(wbf_h[nm], w_h[nm]),
              inc=("dW5", 16))

    for g_in in range(GPT):
        do_stage(0, g_in)
        for nm in ("wk", "wv"):
            for c in (2 * g_in, 2 * g_in + 1):
                ev_wTa = E("sync", lambda e, nm=nm, c=c: e.dma_start(
                    WT[nm][:, c, 0:1024], wbf_h[nm][:, c * 128:(c + 1) * 128],
                    transpose=True),
                    waits=[ev_cast2] if (g_in == 0 and nm == "wk" and c == 0)
                    else (), inc=("dWTa", 16))
    ngroups = ntiles * GPT
    evabs = {0: t_group(0, 0, 0)}
    prev = None
    for ti in range(ntiles):
        s, i = divmod(ti, NT)
        q = ti % 2
        evac4 = None
        for g_in in range(GPT):
            g = i * GPT + g_in
            gg = ti * GPT + g_in
            if gg + 1 < ngroups:
                tj = (gg + 1) // GPT
                evabs[gg + 1] = t_group(tj % 2, (gg + 1) % GPT, gg + 1)
            if ti + 1 < ntiles:
                do_stage(ti + 1, g_in)
            evAB = evabs.pop(gg)
            evac4 = evAB[1]
            p1_proj(q, evAB, g_in, gg)
            if prev is not None:
                p1_kv(prev[0], prev[1], prev[2])
                if prev[1] == NG - 1:
                    ev_ext[prev[0]] = p1_extract(prev[0], h_kv[prev[2]])
                prev = None
            prev = (s, g, gg)
        for c in range(NCH):
            E("gps", lambda e, s=s, i=i, c=c, q=q: e.dma_start(
                xnT_h[s, c, 0:128, i * TT:(i + 1) * TT],
                XF[q][:, c, 16:528]),
                waits=[evac4] if c == 0 else (), inc=("dXN%d" % q, 16))
            st_cnt["dXN%d" % q] += 1
        xf_free[q] = [("pe", em.tot["pe"]),
                      ("dXN%d" % q, 16 * st_cnt["dXN%d" % q])]
        if not pending_casts:
            for nm in tp_sched.get(ti, []):
                if nm in pending_tps:
                    pending_tps.remove(nm)
                    emit_wtp(nm, "dWTb", ("dW5", N5))
    p1_kv(prev[0], prev[1], prev[2])
    ev_ext[prev[0]] = p1_extract(prev[0], h_kv[prev[2]])

    for nm in list(pending_casts):
        pending_casts.remove(nm)
        E("gps", lambda e, nm=nm: e.dma_start(wbf_h[nm], w_h[nm]),
          inc=("dW5", 16))
    for nm in list(pending_tps):
        pending_tps.remove(nm)
        emit_wtp(nm, "dWTb", ("dW5", N5))
    ev_wTb = ("dWTb", em.tot["dWTb"])

    bar = [(nm, em.tot[nm]) for nm in SEMS if em.tot[nm] > 0]
    for eng in Em.ENGINES:
        E(eng, None, waits=bar)

    h_t2 = {}
    h_t3 = {}
    h_rg = {}
    ghalf = [0]
    tps = {}
    mix_ev = {}

    mix_prev = [None]

    def emit_mix(tj, c):
        qj, evtpj = tps[tj]
        ev_d = E("dve", lambda e, qj=qj, c=c: e.tensor_tensor(
            diffT, XF[qj][:, c, 16:528], XF[qj][:, c, 15:527],
            ALU.subtract),
            waits=(list(evtpj) + [ev_mu]) if c == 0 else (),
            inc="dve", fwaits=[mix_prev[0]])
        E("dve", lambda e, qj=qj, c=c: e.scalar_tensor_tensor(
            xr[qj][:, c, 0:512], diffT, mur_sb[:, c:c + 1],
            XF[qj][:, c, 15:527], ALU.mult, ALU.add),
            fwaits=[ev_d])
        ev = E("dve", lambda e, qj=qj, c=c: e.scalar_tensor_tensor(
            xk[qj][:, c, 0:512], diffT, muk_sb[:, c:c + 1],
            XF[qj][:, c, 15:527], ALU.mult, ALU.add),
            inc="dve", fwaits=[ev_d])
        mix_prev[0] = ev
        if c == NCH - 1:
            mix_ev[tj] = ev

    ntiles2 = S * NT
    tps[0] = tile_load_xnt(0, 0)
    for c in range(NCH):
        emit_mix(0, c)

    for s in range(S):
        for i in range(NT):
            ti = s * NT + i
            if ti + 1 < ntiles2:
                s2, i2 = divmod(ti + 1, NT)
                tps[ti + 1] = tile_load_xnt(s2, i2)
            q, ev_tp = tps[ti]
            v_mix = mix_ev[ti]
            v_qf = {}
            for dc in range(NCH):
                bank = dc % 4
                w0 = list(ev_tp) + [ev_wTb, bfree[bank]]
                ev_q = None
                for c in range(NCH):
                    ev_q = E("pe", lambda e, q=q, c=c, dc=dc, bank=bank:
                             e.matmul(
                                 PS[bank][:, :],
                                 WT["wq"][:, c, dc * 128:(dc + 1) * 128],
                                 XF[q][:, c, 16:528],
                                 start=(c == 0), stop=(c == NCH - 1)),
                             waits=w0 if c == 0 else (),
                             inc="pe" if c == NCH - 1 else None)
                hp = dc % 2
                a_ev = E("dve", lambda e, hp=hp, bank=bank: e.tensor_scalar(
                    aT[hp], PS[bank][:, :], 0.0, None, ALU.min),
                    waits=[ev_q], inc="dve")
                e_ev = E("act", lambda e, hp=hp: e.activation(
                    eT[hp], aT[hp], AF.Exp), waits=[a_ev], inc="act")
                v_qf[dc] = E("dve", lambda e, dc=dc, hp=hp, bank=bank:
                             e.scalar_tensor_tensor(
                                 qm[:, dc, 0:512], PS[bank][:, :], 0.0,
                                 eT[hp], ALU.max, ALU.add),
                             waits=[e_ev], inc="dve")
                bfree[bank] = v_qf[dc]
            ev_aend = ("pe", em.tot["pe"])
            ev_den = None
            for dc in range(NCH):
                ev_den = E("pe", lambda e, s=s, dc=dc: e.matmul(
                    PS[6][0:16, :], km[s][:, dc, 0:16], qm[:, dc, 0:512],
                    start=(dc == 0), stop=(dc == NCH - 1)),
                    waits=([v_qf[dc], bfree[6], ev_ext[s]] if dc == 0
                           else [v_qf[dc]]),
                    inc="pe" if dc == NCH - 1 else None)
            a_ka = {}
            for dc in range(NCH):
                bank = dc % 2
                w0 = [v_mix, bfree[bank], ev_wTb]
                ev_k2 = None
                for c in range(NCH):
                    ev_k2 = E("pe", lambda e, q=q, c=c, dc=dc, bank=bank:
                              e.matmul(
                                  PS[bank][:, :],
                                  WT["wkf"][:, c, dc * 128:(dc + 1) * 128],
                                  xk[q][:, c, 0:512],
                                  start=(c == 0), stop=(c == NCH - 1)),
                              waits=w0 if c == 0 else (),
                              inc="pe" if c == NCH - 1 else None)
                ev_rl = E("act", lambda e, dc=dc, bank=bank: e.activation(
                    poolr[dc % 2], PS[bank][:, :], AF.Relu),
                    waits=[ev_k2], inc="act",
                    fwaits=[a_ka.get(dc - 2),
                            h_t2.get(ghalf[0] - 1) if dc < 2 else None])
                a_ka[dc] = E("act", lambda e, dc=dc: e.activation(
                    k_act[:, dc, 0:512], poolr[dc % 2], AF.Square), inc="act",
                    fwaits=[ev_rl])
                bfree[bank] = a_ka[dc]
            ev_eps = E("dve", lambda e: e.tensor_scalar(
                recf, PS[6][0:16, :], 1e-6, None, ALU.add), waits=[ev_den],
                inc="dve")
            ev_rc = E("dve", lambda e: e.reciprocal(recf, recf), inc="dve",
                      fwaits=[ev_eps])
            v_rec = E("dve", lambda e: e.tensor_copy(recb, recf), inc="dve",
                      fwaits=[ev_rc])
            bfree[6] = v_rec
            v_mg = None
            for dc in range(NCH):
                b2 = 2 + dc % 2
                b4 = 4 + dc % 2
                ev_qkv = E("pe", lambda e, s=s, dc=dc, b2=b2: e.matmul(
                    PS[b2][:, :], kvblk[s][:, dc, 0:128], qm[:, dc, 0:512],
                    start=True, stop=True),
                    waits=[bfree[b2], ev_ext[s]], inc="pe")
                ev_pbc = E("pe", lambda e, dc=dc, b4=b4: e.matmul(
                    PS[b4][:, :], bcm_sb[:, dc, 0:128], recb,
                    start=True, stop=True),
                    waits=[v_rec, bfree[b4], ev_mu], inc="pe")
                v_rb = E("dve", lambda e, dc=dc, b4=b4: e.tensor_copy(
                    rb[dc % 2], PS[b4][:, :]), waits=[ev_pbc], inc="dve")
                bfree[b4] = v_rb
                v_mg = E("dve", lambda e, dc=dc, b2=b2: e.tensor_tensor(
                    qm[:, dc, 0:512], PS[b2][:, :], rb[dc % 2], ALU.mult),
                    waits=[ev_qkv], inc="dve", fwaits=[v_rb])
                bfree[b2] = v_mg
            for g_in in range(GPT):
                ga = ti * GPT + g_in
                pa = ga % 2
                g = i * GPT + g_in
                E("gps", lambda e, s=s, g=g: e.dma_start(
                    x2, x_h[s, g * 128:(g + 1) * 128, :]),
                    inc=("dX20", 16))
                st_cnt["dX20"] += 1
                for h in range(2):
                    hi = ghalf[0]
                    ghalf[0] += 1
                    par = hi % 2
                    b_at = 2 + par
                    b_r = 4 + par
                    b_vf = 6 + par
                    ev_r = None
                    w0 = [v_mix, bfree[b_r]]
                    for c in range(NCH):
                        ev_r = E("pe", lambda e, q=q, c=c, g_in=g_in, h=h,
                                 b_r=b_r: e.matmul(
                                     PS[b_r][:, :],
                                     xr[q][:, c, g_in * 128:(g_in + 1) * 128],
                                     WT["wr"][:, c, h * 512:(h + 1) * 512],
                                     start=(c == 0), stop=(c == NCH - 1)),
                                 waits=w0 if c == 0 else (),
                                 inc="pe" if c == NCH - 1 else None)
                    ev_vf = None
                    w0 = [a_ka[NCH - 1], bfree[b_vf]]
                    for c in range(NCH):
                        ev_vf = E("pe", lambda e, c=c, g_in=g_in, h=h,
                                  b_vf=b_vf: e.matmul(
                                      PS[b_vf][:, :],
                                      k_act[:, c, g_in * 128:(g_in + 1) * 128],
                                      WT["wvf"][:, c, h * 512:(h + 1) * 512],
                                      start=(c == 0), stop=(c == NCH - 1)),
                                  waits=w0 if c == 0 else (),
                                  inc="pe" if c == NCH - 1 else None)
                    ev_at = None
                    w0 = [v_mg, bfree[b_at]]
                    for c in range(NCH):
                        ev_at = E("pe", lambda e, c=c, g_in=g_in, h=h,
                                  b_at=b_at: e.matmul(
                                      PS[b_at][:, :],
                                      qm[:, c, g_in * 128:(g_in + 1) * 128],
                                      WT["wo"][:, c, h * 512:(h + 1) * 512],
                                      start=(c == 0), stop=(c == NCH - 1)),
                                  waits=w0 if c == 0 else (),
                                  inc="pe" if c == NCH - 1 else None)
                    ev_th = E("act", lambda e, par=par, b_r=b_r: e.activation(
                        poolr2[par], PS[b_r][:, :], AF.Tanh, scale=0.5),
                        waits=[ev_r, h_t2.get(hi - 2), h_rg.get(hi - 2)],
                        inc="act",
                        fwaits=[a_ka[NCH - 1] if hi % 8 == 0 else None])
                    bfree[b_r] = ev_th
                    ev_rg = E("dve", lambda e, par=par: e.tensor_scalar(
                        poolr2[par], poolr2[par], 0.5, 0.5, ALU.mult, ALU.add),
                        waits=[ev_th], inc="dve")
                    h_rg[hi] = ev_rg
                    h_t2[hi] = E("dve", lambda e, par=par, b_vf=b_vf:
                                 e.tensor_tensor(
                                     t2[par], PS[b_vf][:, :], poolr2[par],
                                     ALU.mult),
                                 waits=[ev_vf], inc="dve", fwaits=[ev_rg])
                    bfree[b_vf] = h_t2[hi]
                    h_t3[(ga, h)] = E("dve", lambda e, pa=pa, h=h, par=par,
                                      b_at=b_at: e.tensor_tensor(
                                          tt3[pa][:, h * 512:(h + 1) * 512],
                                          t2[par], PS[b_at][:, :], ALU.add),
                                      waits=[ev_at,
                                             ("dACC%d" % pa,
                                              16 * st_cnt["dACC%d" % pa])],
                                      inc="dve", fwaits=[h_t2[hi]])
                    bfree[b_at] = h_t3[(ga, h)]
                    if ti + 1 < ntiles2:
                        emit_mix(ti + 1, hi % 8)
                ev_xb = E("gps", lambda e: e.tensor_tensor(
                    x2, x2, bo_bc, ALU.add),
                    waits=[("dX20", 16 * st_cnt["dX20"]), ev_bobc],
                    inc="gps")
                ev_t4 = E("gps", lambda e, pa=pa: e.tensor_tensor(
                    tt3[pa], tt3[pa], x2, ALU.add),
                    waits=[h_t3[(ga, 0)], h_t3[(ga, 1)]], inc="gps",
                    fwaits=[ev_xb])
                E("gps", lambda e, s=s, g=g, pa=pa: e.dma_start(
                    y_h[s, g * 128:(g + 1) * 128, :], tt3[pa]),
                    inc=("dACC%d" % pa, 16), fwaits=[ev_t4])
                st_cnt["dACC%d" % pa] += 1
            xf_free[q] = [ev_aend, v_mix]

    if dbg:
        allsem = [(nm, em.tot[nm]) for nm in ("pe", "dve", "act", "gps")]
        for nm, src in [("qm", qm), ("xr", xr[0]), ("xk", xk[0]),
                        ("k_act", k_act),
                        ("recf", recf), ("kvblk", kvblk[0]), ("km", km[0]),
                        ("WTwkf", WT["wkf"]), ("WTwq", WT["wq"]),
                        ("WTwk", WT["wk"])]:
            E("sync", lambda e, nm=nm, src=src: e.dma_start(dbg_h[nm], src),
              waits=allsem, inc=("dM", 16))

    fin = [("dACC0", em.tot["dACC0"]), ("dACC1", em.tot["dACC1"]),
           ("dM", em.tot["dM"])]
    for eng in Em.ENGINES:
        E(eng, None, waits=fin)

    with nc.Block() as block:
        def mk(eng_name):
            def f(eng):
                waited = {}
                for fn, ws, inc in em.streams[eng_name]:
                    for nm, v in ws:
                        if waited.get(nm, 0) < v:
                            eng.wait_ge(sems[nm], v)
                            waited[nm] = v
                    if fn is None:
                        continue
                    inst = fn(eng)
                    if inc is not None:
                        inst.then_inc(sems[inc[0]], inc[1])
            return f
        block.tensor(mk("pe"))
        block.vector(mk("dve"))
        block.scalar(mk("act"))
        block.gpsimd(mk("gps"))
        block.sync(mk("sync"))
    return nc


_NC_CACHE = {}


def _get_nc(S, T):
    key = (S, T)
    if key not in _NC_CACHE:
        _NC_CACHE[key] = build(S, T)
    return _NC_CACHE[key]


def kernel(L0_attn_x, L0_attn_wq, L0_attn_wk, L0_attn_wv, L0_attn_wo,
           L0_attn_bo, L0_ffn_W_r, L0_ffn_W_k, L0_ffn_W_v, L0_ffn_mu_r,
           L0_ffn_mu_k):
    x = np.ascontiguousarray(np.asarray(L0_attn_x, dtype=np.float32))
    B, T, C = x.shape
    n = 8
    S = B // n
    nc = _get_nc(S, T)

    def w(a):
        return np.ascontiguousarray(np.asarray(a, dtype=np.float32))

    ws = {"wq": w(L0_attn_wq), "wk": w(L0_attn_wk), "wv": w(L0_attn_wv),
          "wo": w(L0_attn_wo), "wr": w(L0_ffn_W_r), "wkf": w(L0_ffn_W_k),
          "wvf": w(L0_ffn_W_v)}
    bo = np.ascontiguousarray(w(L0_attn_bo).reshape(1, C))
    mur = np.ascontiguousarray(w(L0_ffn_mu_r).reshape(C // 128, 128).T)
    muk = np.ascontiguousarray(w(L0_ffn_mu_k).reshape(C // 128, 128).T)
    bcm = np.zeros((16, 8, 128), dtype=ml_dtypes.bfloat16)
    for dc in range(8):
        bcm[2 * dc, dc, 0:64] = 1
        bcm[2 * dc + 1, dc, 64:128] = 1
    ident = np.eye(128, dtype=ml_dtypes.bfloat16)

    in_maps = []
    for k in range(n):
        m = {"x": np.ascontiguousarray(x[k * S:(k + 1) * S])}
        m.update(ws)
        m["bo"] = bo
        m["mur"] = mur
        m["muk"] = muk
        m["bcm"] = bcm
        m["ident"] = ident
        in_maps.append(m)

    res = run_bass_kernel_spmd(nc, in_maps, core_ids=list(range(n)))
    out = np.concatenate([np.asarray(r["y"]) for r in res.results], axis=0)
    return out.astype(np.float32)
